# version 01
# baseline (speedup 1.0000x reference)
import numpy as np
import ml_dtypes
import concourse.bass as bass
import concourse.mybir as mybir
from concourse.bass_utils import run_bass_kernel_spmd
from concourse.replica_groups import maybe_share_collective_output_space

F32 = mybir.dt.float32
F16 = mybir.dt.float16
BF16 = mybir.dt.bfloat16
ALU = mybir.AluOpType
AF = mybir.ActivationFunctionType
AX = mybir.AxisListType

B, L, DM = 2, 8192, 1024
DI, NH, DST, HD = 1024, 32, 64, 32
CONV_DIM = DI + 2 * NH * DST
NPROJ = 2 * DI + 2 * NH * DST + NH
TOK = 2048
QC = 64
TILE = 512
NTILE = TOK // TILE
NCHUNK = TOK // QC
SPT = TILE // 128
MAGIC = float(3 << 21)
NCORE = 8
IFF = 2816
WSG = IFF // NCORE
WSD = DM // NCORE
WPAY_EL = 2 * DM * WSG + IFF * WSD
SPAY_EL = 128 * 512 + 64

ENGS = ("sync", "tensor", "vector", "scalar", "gpsimd")


class Sched:
    def __init__(self):
        self.ops = {e: [] for e in ENGS}
        self.cnt = {}

    def emit(self, eng, fn, inc=None, amt=1):
        if inc is not None:
            self.cnt[inc] = self.cnt.get(inc, 0) + amt
        self.ops[eng].append(("op", fn, inc, amt))
        return self.cnt.get(inc)

    def wait(self, eng, sem_name, val):
        if val and val > 0:
            self.ops[eng].append(("wait", sem_name, val))

    def val(self, sem_name):
        return self.cnt.get(sem_name, 0)


def build_program(stop_after=None, dbg_dump=None):
    DIAG_B1_BARRIER = bool(dbg_dump) and '__b1bar__' in (dbg_dump or [])
    DIAG_NOCS = bool(dbg_dump) and '__nocs__' in (dbg_dump or [])
    DIAG_NOHALO = bool(dbg_dump) and '__nohalo__' in (dbg_dump or [])
    DIAG_SERIALW = bool(dbg_dump) and '__serialw__' in (dbg_dump or [])
    DIAG_FBAR = bool(dbg_dump) and '__fbar__' in (dbg_dump or [])
    DIAG_B1STOP = None
    for _d in (dbg_dump or []):
        if _d.startswith("__b1stop"):
            DIAG_B1STOP = int(_d[8:-2])
    nc = bass.Bass("TRN2", target_bir_lowering=False)

    d_u = nc.dram_tensor("u_slice", [2051, DM], F32, kind="ExternalInput").ap()
    d_wint = nc.dram_tensor("W_inT", [DM, NPROJ], F32, kind="ExternalInput").ap()
    d_woutt = nc.dram_tensor("W_outT", [DI, DM], F32, kind="ExternalInput").ap()
    d_convw = nc.dram_tensor("conv_w", [128, 160], F32, kind="ExternalInput").ap()
    d_convb = nc.dram_tensor("conv_b", [128, 40], F32, kind="ExternalInput").ap()
    d_zb = nc.dram_tensor("zbias_rep", [128, DI], F32, kind="ExternalInput").ap()
    d_drep = nc.dram_tensor("D_rep", [128, NH], F32, kind="ExternalInput").ap()
    d_ident = nc.dram_tensor("in_ident", [128, 128], BF16, kind="ExternalInput").ap()
    d_ident2 = nc.dram_tensor("in_ident2", [128, 64], BF16, kind="ExternalInput").ap()
    d_lt = nc.dram_tensor("in_lt64", [128, 64], F32, kind="ExternalInput").ap()
    d_mask = nc.dram_tensor("in_mask8", [128, 512], BF16, kind="ExternalInput").ap()
    d_ones = nc.dram_tensor("in_ones_row", [1, 128], F32, kind="ExternalInput").ap()
    d_chain = nc.dram_tensor("in_chain", [1, 24], F32, kind="ExternalInput").ap()
    d_wg = nc.dram_tensor("wg_slice", [WSG, DM], F32, kind="ExternalInput").ap()
    d_wu = nc.dram_tensor("wu_slice", [WSG, DM], F32, kind="ExternalInput").ap()
    d_wd = nc.dram_tensor("wd_slice", [WSD, IFF], F32, kind="ExternalInput").ap()
    d_o = nc.dram_tensor("o", [TOK, DM], F32, kind="ExternalOutput").ap()
    d_dbg = nc.dram_tensor("dbg", [128, 16384], F32, kind="ExternalOutput").ap()

    d_winbf = nc.dram_tensor("winbf", [DM, NPROJ], BF16, kind="Internal").ap()
    d_xq = nc.dram_tensor("xq_sp", [8, 128, TOK], BF16, kind="Internal").ap()
    d_cs = nc.dram_tensor("c_keep", [16, 128, TOK], BF16, kind="Internal").ap()
    d_sz = nc.dram_tensor("sz_keep", [16, 128, DM], BF16, kind="Internal").ap()
    d_wcin = nc.dram_tensor("wc_in", [1, WPAY_EL], BF16, kind="Internal").ap()
    sp = maybe_share_collective_output_space("AllGather", [list(range(NCORE))])
    d_wcout = nc.dram_tensor("wc_out", [NCORE, WPAY_EL], BF16, kind="Internal",
                             addr_space=sp).ap()
    d_scin = nc.dram_tensor("sc_in", [1, SPAY_EL], F32, kind="Internal").ap()
    d_scout = nc.dram_tensor("sc_out", [NCORE, SPAY_EL], F32, kind="Internal",
                             addr_space=sp).ap()

    s = Sched()
    stack = []

    def A(h):
        stack.append(h)
        return stack[-1].__enter__()

    def SB(name, shape, dtype):
        return A(nc.sbuf_tensor(name, shape, dtype)).ap()

    ident = SB("ident", [128, 128], BF16)
    ident2 = SB("ident2", [128, 64], BF16)
    lt64 = SB("lt64", [128, 64], F32)
    mask8 = SB("mask8", [128, 512], BF16)
    ones_r = SB("ones_r", [1, 128], F32)
    convw = SB("convw", [128, 160], F32)
    convb = SB("convb", [128, 40], F32)
    zbias = SB("zbias", [128, DI], F32)
    drep = SB("drep", [128, NH], F32)
    chain = SB("chain", [1, 24], F32)
    woutbf = SB("woutbf", [128, 8 * DM], BF16)
    xn_fm = SB("xn_fm", [128, 8 * 2051], BF16)
    epos_all = SB("epos_all", [64, NCHUNK * NH], F32)
    dc0_all = SB("dc0_all", [1, NCHUNK * NH], F32)
    S_f = SB("S_f", [64, 1024], F32)
    S_tmp = SB("S_tmp", [64, 1024], F32)
    S_bf = SB("S_bf", [64, 1024], BF16)
    Sin_f = SB("Sin_f", [64, 1024], F32)
    Sd_bf = SB("Sd_bf", [64, 1024], BF16)
    Ptot = SB("Ptot", [1, NH], F32)
    prow = SB("prow", [1, NH], F32)
    Drun = SB("Drun", [64, 1024], F32)
    dcfac = SB("dcfac", [64, 1024], F32)
    gathP = SB("gathP", [1, NCORE * NH], F32)
    u_t = SB("u_t", [128, 2048], F32)
    gath_j = u_t[:, 0:1024]
    ln_a = SB("ln_a", [128, 16], F32)
    xn_tm = SB("xn_tm", [128, DM], F32)
    xn_tm_bf = SB("xn_tm_bf", [128, DM], BF16)
    xq_tm_bf = SB("xq_tm_bf", [128, DM], BF16)
    w_stream = SB("w_stream", [128, 2 * 512], BF16)
    w_cast = SB("w_cast", [128, 1024], BF16)
    xraw = SB("xraw", [128, 2 * 520], BF16)
    cv_p = SB("cv_p", [128, 4 * 512], BF16)
    bigbf = SB("bigbf", [128, 24576], BF16)
    X_fm = SB("X_fm", [128, 8 * 512], BF16)
    B_fm = bigbf[:, 0:8192]
    C_fm = bigbf[:, 8192:16384]
    B_tm = bigbf[:, 16384:18432]
    B_tm_odd = bigbf[:, 18432:20480]
    B_odd = bigbf[:, 20480:22528]
    C_odd = bigbf[:, 22528:24576]
    Xp_tm = SB("Xp_tm", [128, DI], BF16)
    Xp_odd = SB("Xp_odd", [128, DI], BF16)
    sz_t = SB("sz_t", [128, SPT * DM], BF16)
    sz_odd = SB("sz_odd", [128, DM], BF16)
    dt_t = SB("dt_t", [128, NH], F32)
    dt_lo = SB("dt_lo", [128, NH], F32)
    eneg_s = SB("eneg_s", [128, NH], F32)
    eneg_st = SB("eneg_st", [128, NH], F32)
    yz_fm = SB("yz_fm", [128, 8 * 64], BF16)
    o_sb = SB("o_sb", [128, DM], F32)
    q_am = SB("q_am", [128, 64], F32)
    q_s = SB("q_s", [128, 64], F32)
    q_r = SB("q_r", [128, 64], F32)
    q_sbf = SB("q_sbf", [128, 64], BF16)
    q_t = SB("q_t", [128, 1024], F16)
    q_l1 = SB("q_l1", [128, 1024], F16)
    q_l2 = SB("q_l2", [128, 1024], F16)
    q_m = SB("q_m", [128, 1024], F16)
    q_up = SB("q_up", [128, 1024], F16)
    q_w = SB("q_w", [128, 1024], BF16)
    q_b1 = SB("q_b1", [128, 1024], BF16)
    q_b2 = cv_p[:, 0:1024]
    q_q1 = cv_p[:, 1024:2048]
    qf_t = xn_tm
    qf_l1 = sz_t[:, 0:2048].bitcast(F32)
    qf_l2 = sz_t[:, 2048:4096].bitcast(F32)
    qf_m = X_fm[:, 0:2048].bitcast(F32)
    qf_up = X_fm[:, 2048:4096].bitcast(F32)
    sc_sb_a = q_t[:, 0:1024].bitcast(BF16)
    sc_sb_b = q_l1[:, 0:1024].bitcast(BF16)
    y_sb = q_m[:, 0:1024].bitcast(BF16)
    yD = q_up[:, 0:1024].bitcast(BF16)
    yz_tm = q_w
    wq_nat = bigbf[:, 0:8960]
    hq_fm = bigbf[:, 0:22528]
    wq_st = X_fm[:, 0:2048]
    sg_sb = X_fm[:, 2048:2560]
    hq_bf = X_fm[:, 2560:3072]
    xq_st = X_fm[:, 3072:4096]
    h_f = xraw[:, 0:1024].bitcast(F32)
    gath = bigbf[:, 0:8192].bitcast(F32)
    dC = bigbf[:, 0:2048]
    dy_sb = bigbf[:, 2048:3072]
    dsz = bigbf[:, 3072:4096]

    ps_a = A(nc.psum_tensor("ps_a", [128, 512], F32)).ap()
    ps_b = A(nc.psum_tensor("ps_b", [128, 512], F32)).ap()
    ps_c = A(nc.psum_tensor("ps_c", [128, 512], F32)).ap()
    ps_d = A(nc.psum_tensor("ps_d", [128, 512], F32)).ap()
    ps_tr = A(nc.psum_tensor("ps_tr", [128, 1024], BF16)).ap()
    ps_sc = A(nc.psum_tensor("ps_sc", [128, 512], F32)).ap()
    ps_y1 = A(nc.psum_tensor("ps_y1", [128, 512], F32)).ap()
    ps_y2 = A(nc.psum_tensor("ps_y2", [128, 512], F32)).ap()

    bufs = dict(ln_a=ln_a, xn_fm=xn_fm, xq_st=xq_st, X_fm=X_fm, B_fm=B_fm, C_fm=C_fm,
                B_tm=B_tm, B_odd=B_odd, C_odd=C_odd, Xp_tm=Xp_tm, Xp_odd=Xp_odd,
                sz_t=sz_t, sz_odd=sz_odd, dt_t=dt_t, dt_lo=dt_lo, eneg_s=eneg_s,
                epos_all=epos_all, dc0_all=dc0_all, S_f=S_f, S_bf=S_bf,
                Sin_f=Sin_f, Ptot=Ptot, Drun=Drun, dcfac=dcfac,
                sc_sb_a=sc_sb_a, sc_sb_b=sc_sb_b, y_sb=y_sb, yD=yD,
                yz_tm=yz_tm, yz_fm=yz_fm, o_sb=o_sb, u_t=u_t, h_f=h_f,
                hq_bf=hq_bf, hq_fm=hq_fm, wq_nat=wq_nat, dC=dC, dy_sb=dy_sb,
                dsz=dsz, Sd_bf=Sd_bf, sg_sb=sg_sb, wq_st=wq_st,
                q_am=q_am, q_s=q_s, q_r=q_r, q_t=q_t,
                q_w=q_w, q_b2=q_b2, xn_tm=xn_tm, xn_tm_bf=xn_tm_bf,
                xq_tm_bf=xq_tm_bf, w_cast=w_cast, xraw=xraw, cv_p=cv_p,
                mask8=mask8, lt64=lt64, woutbf=woutbf)
    sems = {}
    for name in ("pe", "dve", "act", "gps", "snc", "dq_s", "dq_g",
                 "dq0", "dq1", "dq2", "dq3"):
        sems[name] = A(nc.semaphore("sem_" + name))

    blk = A(nc.Block())

    def dma_s(out, in_, sem="dq_s"):
        return s.emit("sync", lambda e, o=out, i=in_: e.dma_start(out=o, in_=i),
                      inc=sem, amt=16)

    def dma_g(out, in_, sem="dq_g"):
        return s.emit("gpsimd", lambda e, o=out, i=in_: e.dma_start(out=o, in_=i),
                      inc=sem, amt=16)

    def barrier():
        s.wait("sync", "dq_s", s.val("dq_s"))
        for nm in ("dq0", "dq1", "dq2", "dq3"):
            s.wait("sync", nm, s.val(nm))
        s.emit("sync", lambda e: e.nop(), inc="snc")
        s.wait("gpsimd", "dq_g", s.val("dq_g"))
        s.emit("gpsimd", lambda e: e.nop(), inc="gps")
        s.emit("tensor", lambda e: e.nop(), inc="pe")
        s.emit("vector", lambda e: e.nop(), inc="dve")
        s.emit("scalar", lambda e: e.nop(), inc="act")
        tgt = {k: s.val(k) for k in ("snc", "gps", "pe", "dve", "act")}
        for eng, own in (("sync", "snc"), ("gpsimd", "gps"), ("tensor", "pe"),
                         ("vector", "dve"), ("scalar", "act")):
            for k, v in tgt.items():
                if k != own:
                    s.wait(eng, k, v)

    def quant_pipe(src_ap, dst_ap, ncols, f16):
        nb = ncols // 16
        (tq, l1, l2, mq, upq) = ((q_t, q_l1, q_l2, q_m, q_up) if f16 else
                                 (qf_t, qf_l1, qf_l2, qf_m, qf_up))
        v3 = src_ap.rearrange("p (b k) -> p b k", k=16)
        s.emit("vector", lambda e: e.tensor_reduce(
            out=q_am[:, 0:nb], in_=v3, axis=AX.X, op=ALU.max,
            apply_absolute_value=True), inc="dve")
        s.emit("vector", lambda e: e.tensor_scalar(
            out=q_s[:, 0:nb], in0=q_am[:, 0:nb], scalar1=1.0 / 6.0,
            scalar2=1e-12, op0=ALU.mult, op1=ALU.max), inc="dve")
        s.emit("vector", lambda e: e.reciprocal(out=q_r[:, 0:nb],
                                                in_=q_s[:, 0:nb]), inc="dve")
        s.emit("vector", lambda e: e.tensor_scalar(
            out=q_sbf[:, 0:nb], in0=q_s[:, 0:nb], scalar1=1.0, scalar2=None,
            op0=ALU.mult), inc="dve")
        rv = q_r[:, 0:nb].unsqueeze(2).broadcast_to([128, nb, 16])
        s.emit("vector", lambda e: e.tensor_tensor(
            out=tq[:, 0:ncols].rearrange("p (b k) -> p b k", k=16),
            in0=v3, in1=rv, op=ALU.mult), inc="dve")
        vt = s.val("dve")
        s.wait("scalar", "dve", vt)
        s.emit("scalar", lambda e: e.activation(out=l1[:, 0:ncols], in_=tq[:, 0:ncols],
                                                func=AF.Copy, bias=1.0, scale=0.5),
               inc="act")
        s.emit("scalar", lambda e: e.activation(out=l2[:, 0:ncols], in_=tq[:, 0:ncols],
                                                func=AF.Copy, bias=2.0, scale=0.25),
               inc="act")
        s.wait("vector", "act", s.val("act"))
        s.emit("vector", lambda e: e.tensor_tensor(out=mq[:, 0:ncols], in0=l1[:, 0:ncols],
                                                   in1=l2[:, 0:ncols], op=ALU.min),
               inc="dve")
        s.emit("vector", lambda e: e.tensor_tensor(out=upq[:, 0:ncols], in0=tq[:, 0:ncols],
                                                   in1=mq[:, 0:ncols], op=ALU.min),
               inc="dve")
        s.wait("scalar", "dve", s.val("dve"))
        s.emit("scalar", lambda e: e.activation(out=l1[:, 0:ncols], in_=upq[:, 0:ncols],
                                                func=AF.Copy, bias=-1.0, scale=0.5),
               inc="act")
        s.emit("scalar", lambda e: e.activation(out=l2[:, 0:ncols], in_=upq[:, 0:ncols],
                                                func=AF.Copy, bias=-2.0, scale=0.25),
               inc="act")
        s.wait("vector", "act", s.val("act"))
        s.emit("vector", lambda e: e.tensor_tensor(out=mq[:, 0:ncols], in0=l1[:, 0:ncols],
                                                   in1=l2[:, 0:ncols], op=ALU.max),
               inc="dve")
        s.emit("vector", lambda e: e.tensor_tensor(out=tq[:, 0:ncols], in0=upq[:, 0:ncols],
                                                   in1=mq[:, 0:ncols], op=ALU.max),
               inc="dve")
        s.emit("vector", lambda e: e.tensor_scalar(out=q_w[:, 0:ncols], in0=tq[:, 0:ncols],
                                                   scalar1=MAGIC, scalar2=MAGIC,
                                                   op0=ALU.add, op1=ALU.subtract),
               inc="dve")
        s.wait("scalar", "dve", s.val("dve"))
        s.emit("scalar", lambda e: e.activation(out=q_b1[:, 0:ncols], in_=q_w[:, 0:ncols],
                                                func=AF.Copy, bias=-2.0, scale=2.0),
               inc="act")
        s.emit("scalar", lambda e: e.activation(out=q_b2[:, 0:ncols], in_=q_w[:, 0:ncols],
                                                func=AF.Copy, bias=-8.0, scale=4.0),
               inc="act")
        s.wait("vector", "act", s.val("act"))
        qsc = q_m[:, 0:1024].bitcast(BF16)
        s.emit("vector", lambda e, qsc=qsc: e.tensor_tensor(out=qsc[:, 0:ncols],
                                                   in0=q_b1[:, 0:ncols],
                                                   in1=q_b2[:, 0:ncols], op=ALU.max),
               inc="dve")
        s.emit("vector", lambda e, qsc=qsc: e.tensor_tensor(out=q_q1[:, 0:ncols],
                                                   in0=q_w[:, 0:ncols],
                                                   in1=qsc[:, 0:ncols], op=ALU.max),
               inc="dve")
        s.wait("scalar", "dve", s.val("dve"))
        s.emit("scalar", lambda e: e.activation(out=q_b1[:, 0:ncols], in_=q_q1[:, 0:ncols],
                                                func=AF.Copy, bias=2.0, scale=2.0),
               inc="act")
        s.emit("scalar", lambda e: e.activation(out=q_b2[:, 0:ncols], in_=q_q1[:, 0:ncols],
                                                func=AF.Copy, bias=8.0, scale=4.0),
               inc="act")
        s.wait("vector", "act", s.val("act"))
        s.emit("vector", lambda e, qsc=qsc: e.tensor_tensor(out=qsc[:, 0:ncols],
                                                   in0=q_b1[:, 0:ncols],
                                                   in1=q_b2[:, 0:ncols], op=ALU.min),
               inc="dve")
        s.emit("vector", lambda e, qsc=qsc: e.tensor_tensor(out=q_b2[:, 0:ncols],
                                                   in0=q_q1[:, 0:ncols],
                                                   in1=qsc[:, 0:ncols], op=ALU.min),
               inc="dve")
        sv = q_sbf[:, 0:nb].unsqueeze(2).broadcast_to([128, nb, 16])
        s.emit("vector", lambda e: e.tensor_tensor(
            out=dst_ap.rearrange("p (b k) -> p b k", k=16),
            in0=q_b2[:, 0:ncols].rearrange("p (b k) -> p b k", k=16),
            in1=sv, op=ALU.mult), inc="dve")
        return s.val("dve")

    for out, in_ in ((ident, d_ident), (ident2, d_ident2), (lt64, d_lt), (mask8, d_mask),
                     (ones_r, d_ones), (convw, d_convw), (convb, d_convb),
                     (zbias, d_zb), (drep, d_drep), (chain, d_chain)):
        dma_s(out[:], in_[:])
    s.wait("vector", "dq_s", s.val("dq_s"))

    ut_guard = [0, 0]
    for kc in range(8):
        slot = kc % 2
        sl_sem = "dq%d" % slot
        s.wait("sync", "dve", ut_guard[slot])
        dma_s(u_t[:, slot * DM: slot * DM + DM], d_woutt[kc * 128:(kc + 1) * 128, :],
              sem=sl_sem)
        s.wait("vector", sl_sem, s.val(sl_sem))
        s.emit("vector", lambda e, kc=kc, slot=slot: e.tensor_scalar(
            out=woutbf[:, kc * DM:(kc + 1) * DM], in0=u_t[:, slot * DM: slot * DM + DM],
            scalar1=1.0, scalar2=None, op0=ALU.mult), inc="dve")
        ut_guard[slot] = s.val("dve")

    colblocks = [(i * 1024, 1024) for i in range(6)] + [(6144, 32)]
    wc_guard = 0
    for kc in range(8):
        for (c0, cn) in colblocks:
            slot = 0
            s.wait("sync", "dve", ut_guard[0])
            dma_s(u_t[:, 0:cn], d_wint[kc * 128:(kc + 1) * 128, c0:c0 + cn],
                  sem="dq0")
            s.wait("vector", "dq0", s.val("dq0"))
            s.wait("vector", "dq_g", wc_guard)
            s.emit("vector", lambda e, cn=cn: e.tensor_scalar(
                out=w_cast[:, 0:cn], in0=u_t[:, 0:cn], scalar1=1.0, scalar2=None,
                op0=ALU.mult), inc="dve")
            ut_guard[0] = s.val("dve")
            s.wait("gpsimd", "dve", s.val("dve"))
            dma_g(d_winbf[kc * 128:(kc + 1) * 128, c0:c0 + cn], w_cast[:, 0:cn])
            wc_guard = s.val("dq_g")
    v_winbf = s.val("dq_g")

    phase_marks = {}

    def mark(p):
        phase_marks[p] = {e: len(s.ops[e]) for e in ENGS}

    mark("W")
    def wq_prep(d_src, rows, cols, nat_off):
        nparts = (rows + 127) // 128
        for pi in range(nparts):
            rn = min(128, rows - pi * 128)
            for cb in range(0, cols, 1024):
                cn = min(1024, cols - cb)
                s.wait("sync", "dve", ut_guard[0])
                if rn < 128:
                    s.emit("vector", lambda e, cn=cn: e.memset(u_t[:, 0:cn], 0.0),
                           inc="dve")
                    s.wait("sync", "dve", s.val("dve"))
                dma_s(u_t[0:rn, 0:cn], d_src[pi * 128:pi * 128 + rn, cb:cb + cn])
                s.wait("vector", "dq_s", s.val("dq_s"))
                quant_pipe(u_t[:, 0:cn],
                           wq_nat[:, nat_off + pi * cols + cb:
                                  nat_off + pi * cols + cb + cn], cn, f16=False)
                ut_guard[0] = s.val("dve")

    wq_prep(d_wg, WSG, DM, 0)
    wq_prep(d_wu, WSG, DM, 3 * DM)
    wq_prep(d_wd, WSD, IFF, 6 * DM)
    v_wqnat = s.val("dve")
    s.wait("tensor", "dve", v_wqnat)

    guard_dve = [0]
    guard_dma = [wc_guard]

    def wq_transpose(nat_off, rows, cols, pay_off, pay_cols):
        nparts = (rows + 127) // 128
        for pi in range(nparts):
            rn = min(128, rows - pi * 128)
            for cb in range(0, cols, 128):
                s.wait("tensor", "dve", guard_dve[0])
                s.emit("tensor", lambda e, pi=pi, cb=cb: e.transpose(
                    out=ps_tr[:, 0:128],
                    in_=wq_nat[:, nat_off + pi * cols + cb:
                               nat_off + pi * cols + cb + 128],
                    identity=ident), inc="pe")
                s.wait("vector", "pe", s.val("pe"))
                s.wait("vector", "dq_g", guard_dma[0])
                s.emit("vector", lambda e, rn=rn: e.tensor_copy(
                    out=w_cast[:, 0:rn], in_=ps_tr[:, 0:rn]), inc="dve")
                guard_dve[0] = s.val("dve")
                s.wait("gpsimd", "dve", s.val("dve"))
                dma_g(d_wcin[0, pay_off:pay_off + cols * pay_cols].rearrange(
                    "(k w) -> k w", w=pay_cols)[cb:cb + 128, pi * 128:pi * 128 + rn],
                    w_cast[:, 0:rn])
                guard_dma[0] = s.val("dq_g")

    wq_transpose(0, WSG, DM, 0, WSG)
    wq_transpose(3 * DM, WSG, DM, DM * WSG, WSG)
    wq_transpose(6 * DM, WSD, IFF, 2 * DM * WSG, WSD)
    s.wait("gpsimd", "dq_g", s.val("dq_g"))
    s.emit("gpsimd", lambda e: e.collective_compute(
        "AllGather", ALU.bypass, replica_groups=[list(range(NCORE))],
        ins=[d_wcin[:]], outs=[d_wcout[:]]), inc="gps")
    v_wq_gathered = s.val("gps")

    barrier()

    mark("A")
    def ln_subtile(row0, nrows):
        s.wait("sync", "pe", s.val("pe"))
        dma_s(u_t[0:nrows, 0:DM], d_u[row0:row0 + nrows, :])
        s.wait("vector", "dq_s", s.val("dq_s"))
        ut = u_t[0:nrows, 0:DM]
        s.emit("vector", lambda e: e.reduce_sum(out=ln_a[0:nrows, 0:1], in_=ut,
                                                axis=AX.X), inc="dve")
        s.wait("scalar", "dq_s", s.val("dq_s"))
        s.wait("scalar", "dve", s.val("dve"))
        s.emit("scalar", lambda e: e.activation(out=xn_tm[0:nrows, 0:DM], in_=ut,
                                                func=AF.Square), inc="act")
        s.wait("vector", "act", s.val("act"))
        s.emit("vector", lambda e: e.reduce_sum(out=ln_a[0:nrows, 1:2],
                                                in_=xn_tm[0:nrows, 0:DM],
                                                axis=AX.X), inc="dve")
        s.emit("vector", lambda e: e.tensor_scalar(
            out=ln_a[0:nrows, 2:3], in0=ln_a[0:nrows, 0:1], scalar1=1.0 / DM,
            scalar2=None, op0=ALU.mult), inc="dve")
        s.emit("vector", lambda e: e.tensor_tensor(
            out=ln_a[0:nrows, 3:4], in0=ln_a[0:nrows, 2:3], in1=ln_a[0:nrows, 2:3],
            op=ALU.mult), inc="dve")
        s.emit("vector", lambda e: e.tensor_scalar(
            out=ln_a[0:nrows, 4:5], in0=ln_a[0:nrows, 1:2], scalar1=1.0 / DM,
            scalar2=None, op0=ALU.mult), inc="dve")
        s.emit("vector", lambda e: e.tensor_tensor(
            out=ln_a[0:nrows, 5:6], in0=ln_a[0:nrows, 4:5], in1=ln_a[0:nrows, 3:4],
            op=ALU.subtract), inc="dve")
        s.emit("vector", lambda e: e.tensor_scalar(
            out=ln_a[0:nrows, 6:7], in0=ln_a[0:nrows, 5:6], scalar1=1.0,
            scalar2=1e-5, op0=ALU.mult, op1=ALU.add), inc="dve")
        s.wait("scalar", "dve", s.val("dve"))
        s.emit("scalar", lambda e: e.activation(out=ln_a[0:nrows, 7:8],
                                                in_=ln_a[0:nrows, 6:7],
                                                func=AF.Sqrt), inc="act")
        s.wait("vector", "act", s.val("act"))
        s.emit("vector", lambda e: e.reciprocal(out=ln_a[0:nrows, 8:9],
                                                in_=ln_a[0:nrows, 7:8]), inc="dve")
        s.emit("vector", lambda e: e.tensor_tensor(
            out=ln_a[0:nrows, 9:10], in0=ln_a[0:nrows, 2:3],
            in1=ln_a[0:nrows, 8:9], op=ALU.mult), inc="dve")
        s.emit("vector", lambda e: e.tensor_scalar(
            out=ln_a[0:nrows, 10:11], in0=ln_a[0:nrows, 9:10], scalar1=-1.0,
            scalar2=None, op0=ALU.mult), inc="dve")
        s.emit("vector", lambda e: e.tensor_scalar(
            out=xn_tm[0:nrows, 0:DM], in0=ut, scalar1=ln_a[0:nrows, 8:9],
            scalar2=ln_a[0:nrows, 10:11], op0=ALU.mult, op1=ALU.add), inc="dve")
        s.emit("vector", lambda e: e.tensor_scalar(
            out=xn_tm_bf[0:nrows, 0:DM], in0=xn_tm[0:nrows, 0:DM], scalar1=1.0,
            scalar2=None, op0=ALU.mult), inc="dve")

    ln_subtile(0, 3)
    s.wait("tensor", "dve", s.val("dve"))
    for kc in range(8):
        s.emit("tensor", lambda e, kc=kc: e.transpose(
            out=ps_tr[:, 0:128], in_=xn_tm_bf[0:3, kc * 128:(kc + 1) * 128],
            identity=ident[0:3, :]), inc="pe")
        s.wait("vector", "pe", s.val("pe"))
        s.emit("vector", lambda e, kc=kc: e.tensor_copy(
            out=xn_fm[:, kc * 2051: kc * 2051 + 3], in_=ps_tr[:, 0:3]), inc="dve")
        s.wait("tensor", "dve", s.val("dve"))

    for st in range(16):
        mark("A.%d" % st)
        ln_subtile(3 + st * 128, 128)
        quant_pipe(xn_tm[:, 0:DM], xq_tm_bf[:, 0:DM], DM, f16=True)
        s.wait("tensor", "dve", s.val("dve"))
        for kc in range(8):
            s.emit("tensor", lambda e, kc=kc: e.transpose(
                out=ps_tr[:, 0:128], in_=xn_tm_bf[:, kc * 128:(kc + 1) * 128],
                identity=ident), inc="pe")
            s.wait("vector", "pe", s.val("pe"))
            s.emit("vector", lambda e, kc=kc, st=st: e.tensor_copy(
                out=xn_fm[:, kc * 2051 + 3 + st * 128: kc * 2051 + 3 + st * 128 + 128],
                in_=ps_tr[:, 0:128]), inc="dve")
            s.wait("tensor", "dve", s.val("dve"))
        for kc in range(8):
            s.emit("tensor", lambda e, kc=kc: e.transpose(
                out=ps_tr[:, 0:128], in_=xq_tm_bf[:, kc * 128:(kc + 1) * 128],
                identity=ident), inc="pe")
            s.wait("vector", "pe", s.val("pe"))
            s.wait("vector", "dq_g", s.val("dq_g"))
            s.emit("vector", lambda e, kc=kc, st=st: e.tensor_copy(
                out=w_cast[:, 0:128], in_=ps_tr[:, 0:128]), inc="dve")
            s.wait("gpsimd", "dve", s.val("dve"))
            dma_g(d_xq[kc, :, st * 128:(st + 1) * 128], w_cast[:, 0:128])
            s.wait("tensor", "dve", s.val("dve"))

    s.emit("vector", lambda e: e.memset(S_f[:, :], 0.0), inc="dve")
    s.emit("vector", lambda e: e.memset(S_bf[:, :], 0.0), inc="dve")
    s.emit("vector", lambda e: e.memset(Ptot[0:1, :], 1.0), inc="dve")
    barrier()
    s.wait("sync", "dq_g", v_winbf)

    mark("B")
    ws_guard = {}
    for tile in range(NTILE):
        t0 = tile * TILE
        for _sl in range(4):
            ws_guard[("b1", _sl)] = max(ws_guard.get(("b1", _sl), 0), s.val("pe"))
        last_evac = 0
        for mc in range(40):
            if DIAG_B1STOP is not None and mc > DIAG_B1STOP:
                break
            pa = ps_a if (mc % 2 == 0) else ps_b
            for kq in range(8):
                slot = kq % 4
                key = ("b1", slot)
                sl_sem = "dq%d" % slot
                s.wait("sync", "pe", s.val("pe") if DIAG_SERIALW else ws_guard.get(key, 0))
                dma_s(w_stream[:, slot * 128: slot * 128 + 128],
                      d_winbf[kq * 128:(kq + 1) * 128, mc * 128:(mc + 1) * 128],
                      sem=sl_sem)
                s.wait("tensor", sl_sem, s.val(sl_sem))
                if kq == 0:
                    s.wait("tensor", "dve", last_evac)
                s.emit("tensor", lambda e, slot=slot, kq=kq, pa=pa, t0=t0: e.matmul(
                    out=pa[:, 0:512], lhsT=w_stream[:, slot * 128: slot * 128 + 128],
                    rhs=xn_fm[:, kq * 2051 + t0 + 3: kq * 2051 + t0 + 3 + 512],
                    start=(kq == 0), stop=(kq == 7)), inc="pe")
                if not DIAG_NOHALO:
                    s.emit("tensor", lambda e, slot=slot, kq=kq, mc=mc, t0=t0: e.matmul(
                        out=ps_d[:, 3 * mc: 3 * mc + 3],
                        lhsT=w_stream[:, slot * 128: slot * 128 + 128],
                        rhs=xn_fm[:, kq * 2051 + t0: kq * 2051 + t0 + 3],
                        start=(kq == 0), stop=(kq == 7)), inc="pe")
                ws_guard[key] = s.val("pe")
            vmm = s.val("pe")
            xr = xraw[:, (mc % 2) * 520: (mc % 2) * 520 + 520]
            s.wait("vector", "pe", vmm)
            s.wait("vector", "act", s.val("act"))
            s.emit("vector", lambda e, xr=xr, pa=pa: e.tensor_copy(
                out=xr[:, 3:515], in_=pa[:, 0:512]), inc="dve")
            s.emit("vector", lambda e, xr=xr, mc=mc: e.tensor_copy(
                out=xr[:, 0:3], in_=ps_d[:, 3 * mc:3 * mc + 3]), inc="dve")
            last_evac = s.val("dve")
            s.wait("scalar", "dve", s.val("dve"))
            cw0 = convw[:, mc * 4: mc * 4 + 1]
            cw1 = convw[:, mc * 4 + 1: mc * 4 + 2]
            cw2 = convw[:, mc * 4 + 2: mc * 4 + 3]
            cw3 = convw[:, mc * 4 + 3: mc * 4 + 4]
            cbm = convb[:, mc:mc + 1]
            s.emit("scalar", lambda e, xr=xr, cw0=cw0: e.activation(
                out=cv_p[:, 0:512], in_=xr[:, 0:512], func=AF.Copy,
                bias=0.0, scale=cw0), inc="act")
            s.emit("scalar", lambda e, xr=xr, cw1=cw1: e.activation(
                out=cv_p[:, 512:1024], in_=xr[:, 1:513], func=AF.Copy,
                bias=0.0, scale=cw1), inc="act")
            va = s.val("act")
            s.emit("vector", lambda e, xr=xr, cw2=cw2, cbm=cbm: e.tensor_scalar(
                out=cv_p[:, 1024:1536], in0=xr[:, 2:514], scalar1=cw2,
                scalar2=cbm, op0=ALU.mult, op1=ALU.add), inc="dve")
            s.emit("vector", lambda e, xr=xr, cw3=cw3: e.tensor_scalar(
                out=cv_p[:, 1536:2048], in0=xr[:, 3:515], scalar1=cw3,
                scalar2=None, op0=ALU.mult), inc="dve")
            s.wait("vector", "act", va)
            xr_o = xraw[:, ((mc + 1) % 2) * 520: ((mc + 1) % 2) * 520 + 512]
            s.emit("vector", lambda e, xr_o=xr_o: e.tensor_tensor(
                out=xr_o, in0=cv_p[:, 0:512], in1=cv_p[:, 512:1024],
                op=ALU.add), inc="dve")
            s.emit("vector", lambda e: e.tensor_tensor(
                out=cv_p[:, 0:512], in0=cv_p[:, 1024:1536],
                in1=cv_p[:, 1536:2048], op=ALU.add), inc="dve")
            if mc < 8:
                dstap = X_fm[:, mc * 512:(mc + 1) * 512]
            elif mc < 24:
                dstap = B_fm[:, (mc - 8) * 512:(mc - 7) * 512]
            else:
                dstap = C_fm[:, (mc - 24) * 512:(mc - 23) * 512]
            s.emit("vector", lambda e, dstap=dstap, xr_o=xr_o: e.tensor_tensor(
                out=dstap, in0=xr_o, in1=cv_p[:, 0:512],
                op=ALU.add), inc="dve")
            if mc >= 24 and not DIAG_NOCS:
                s.wait("gpsimd", "dve", s.val("dve"))
                dma_g(d_cs[mc - 24, :, t0:t0 + 512], dstap)
            if DIAG_B1_BARRIER:
                barrier()
        mark("B1.%d" % tile)
        for _sl in range(2):
            ws_guard[("b2", _sl)] = max(ws_guard.get(("b2", _sl), 0), s.val("pe"))
        v_conv = s.val("dve")
        for sub in range(SPT):
            stok = t0 + sub * 128
            gch = stok // QC
            for (part, c0, cn) in ((0, 5120, 512), (1, 5632, 512), (2, 6144, 32)):
                for kq in range(8):
                    slot = kq % 2
                    key = ("b2", slot)
                    sl_sem = "dq%d" % slot
                    s.wait("sync", "pe", ws_guard.get(key, 0))
                    dma_s(w_stream[:, slot * 512: slot * 512 + cn],
                          d_winbf[kq * 128:(kq + 1) * 128, c0:c0 + cn],
                          sem=sl_sem)
                    s.wait("tensor", sl_sem, s.val(sl_sem))
                    s.emit("tensor", lambda e, slot=slot, kq=kq, cn=cn, stok=stok: e.matmul(
                        out=ps_c[:, 0:cn],
                        lhsT=xn_fm[:, kq * 2051 + 3 + stok: kq * 2051 + 3 + stok + 128],
                        rhs=w_stream[:, slot * 512: slot * 512 + cn],
                        start=(kq == 0), stop=(kq == 7)), inc="pe")
                    ws_guard[key] = s.val("pe")
                vmm = s.val("pe")
                if part < 2:
                    s.wait("vector", "pe", vmm)
                    s.emit("vector", lambda e, part=part: e.tensor_tensor(
                        out=h_f[:, 0:512], in0=ps_c[:, 0:512],
                        in1=zbias[:, part * 512:(part + 1) * 512], op=ALU.add),
                        inc="dve")
                    s.wait("scalar", "dve", s.val("dve"))
                    s.emit("scalar", lambda e, sub=sub, part=part: e.activation(
                        out=sz_t[:, sub * DM + part * 512: sub * DM + part * 512 + 512],
                        in_=h_f[:, 0:512], func=AF.Silu), inc="act")
                    s.wait("tensor", "act", s.val("act"))
                else:
                    s.wait("scalar", "pe", vmm)
                    s.emit("scalar", lambda e: e.activation(
                        out=h_f[:, 0:32], in_=ps_c[:, 0:32], func=AF.Exp), inc="act")
                    s.wait("vector", "act", s.val("act"))
                    s.emit("vector", lambda e: e.tensor_scalar(
                        out=h_f[:, 32:64], in0=h_f[:, 0:32], scalar1=1.0,
                        scalar2=1.0, op0=ALU.mult, op1=ALU.add), inc="dve")
                    s.wait("scalar", "dve", s.val("dve"))
                    s.emit("scalar", lambda e: e.activation(
                        out=dt_t[:, 0:NH], in_=h_f[:, 32:64], func=AF.Ln), inc="act")
                    s.wait("tensor", "act", s.val("act"))
            s.wait("gpsimd", "act", s.val("act"))
            dma_g(d_sz[stok // 128, :, 0:DM], sz_t[:, sub * DM:(sub + 1) * DM])
            dma_g(sz_odd[0:64, 0:DM], sz_t[64:128, sub * DM:(sub + 1) * DM])
            dma_g(dt_lo[0:64, 0:NH], dt_t[64:128, 0:NH])
            v_szr = s.val("dq_g")
            s.wait("tensor", "dq_g", v_szr)
            s.wait("tensor", "act", s.val("act"))
            for ci in range(2):
                cg = gch + ci
                rhs_dt = dt_t[0:64, 0:NH] if ci == 0 else dt_lo[0:64, 0:NH]
                s.emit("tensor", lambda e, rhs_dt=rhs_dt, ci=ci: e.matmul(
                    out=ps_sc[0:64, ci * 64: ci * 64 + 32], lhsT=lt64[0:64, :],
                    rhs=rhs_dt, start=True, stop=True), inc="pe")
            s.wait("scalar", "pe", s.val("pe"))
            for ci in range(2):
                cg = gch + ci
                s.emit("scalar", lambda e, ci=ci, cg=cg: e.activation(
                    out=epos_all[0:64, cg * NH:(cg + 1) * NH],
                    in_=ps_sc[0:64, ci * 64: ci * 64 + 32], func=AF.Exp,
                    scale=-1.0), inc="act")
            s.emit("scalar", lambda e: e.activation(
                out=eneg_s[0:64, 0:NH], in_=ps_sc[0:64, 0:32], func=AF.Exp,
                scale=1.0), inc="act")
            s.emit("scalar", lambda e: e.activation(
                out=eneg_st[0:64, 0:NH], in_=ps_sc[0:64, 64:96], func=AF.Exp,
                scale=1.0), inc="act")
            va_e = s.val("act")
            s.wait("gpsimd", "act", va_e)
            dma_g(eneg_s[64:128, 0:NH], eneg_st[0:64, 0:NH])
            dma_g(dc0_all[0:1, gch * NH:(gch + 1) * NH],
                  epos_all[63:64, gch * NH:(gch + 1) * NH])
            dma_g(dc0_all[0:1, (gch + 1) * NH:(gch + 2) * NH],
                  epos_all[63:64, (gch + 1) * NH:(gch + 2) * NH])
            v_eneg = s.val("dq_g")
            s.wait("tensor", "act", va_e)
            s.wait("tensor", "dve", v_conv)
            s.wait("vector", "dq_g", v_eneg)
            env = eneg_s[:, 0:NH].unsqueeze(2).broadcast_to([128, NH, HD])
            for fc in range(8):
                s.emit("tensor", lambda e, fc=fc, sub=sub: e.transpose(
                    out=ps_tr[:, 0:128],
                    in_=X_fm[:, fc * 512 + sub * 128: fc * 512 + sub * 128 + 128],
                    identity=ident), inc="pe")
                s.wait("vector", "pe", s.val("pe"))
                ev = eneg_s[:, fc * 4:(fc + 1) * 4].unsqueeze(2).broadcast_to(
                    [128, 4, HD])
                s.emit("vector", lambda e, fc=fc, ev=ev: e.tensor_tensor(
                    out=Xp_tm[:, fc * 128:(fc + 1) * 128].rearrange(
                        "p (h d) -> p h d", d=HD),
                    in0=ps_tr[:, 0:128].rearrange("p (h d) -> p h d", d=HD),
                    in1=ev, op=ALU.mult), inc="dve")
                s.wait("tensor", "dve", s.val("dve"))
            for fc in range(16):
                s.emit("tensor", lambda e, fc=fc, sub=sub: e.transpose(
                    out=ps_tr[:, 0:128],
                    in_=B_fm[:, fc * 512 + sub * 128: fc * 512 + sub * 128 + 128],
                    identity=ident), inc="pe")
                s.wait("vector", "pe", s.val("pe"))
                s.emit("vector", lambda e, fc=fc: e.tensor_copy(
                    out=B_tm[:, fc * 128:(fc + 1) * 128], in_=ps_tr[:, 0:128]),
                    inc="dve")
                s.wait("tensor", "dve", s.val("dve"))
            v_tr = s.val("dve")
            s.wait("gpsimd", "dve", v_tr)
            for j in range(16):
                dma_g(B_odd[0:64, j * 128:(j + 1) * 128],
                      B_fm[64:128, j * 512 + sub * 128: j * 512 + sub * 128 + 128])
                dma_g(C_odd[0:64, j * 128:(j + 1) * 128],
                      C_fm[64:128, j * 512 + sub * 128: j * 512 + sub * 128 + 128])
            dma_g(Xp_odd[0:64, 0:DI], Xp_tm[64:128, 0:DI])
            dma_g(B_tm_odd[0:64, 0:2048], B_tm[64:128, 0:2048])
            v_ra = s.val("dq_g")
            s.wait("tensor", "dq_g", v_ra)

            for ci in range(2):
                po = ci * 64
                cg = gch + ci
                clt = sub * 128 + po
                for h in range(NH):
                    if h % 2 == 0:
                        lb = B_fm[0:64, (h // 2) * 512 + clt:(h // 2) * 512 + clt + 64]
                        rc = C_fm[0:64, (h // 2) * 512 + clt:(h // 2) * 512 + clt + 64]
                    else:
                        lb = B_odd[0:64, (h // 2) * 128 + po:(h // 2) * 128 + po + 64]
                        rc = C_odd[0:64, (h // 2) * 128 + po:(h // 2) * 128 + po + 64]
                    s.emit("tensor", lambda e, h=h, lb=lb, rc=rc: e.matmul(
                        out=ps_sc[0:64, (h % 8) * 64:(h % 8) * 64 + 64],
                        lhsT=lb, rhs=rc, start=True, stop=True), inc="pe")
                    if h % 8 == 7:
                        s.wait("vector", "pe", s.val("pe"))
                        grp = h // 8
                        dst_sc = (sc_sb_a if grp < 2 else sc_sb_b)[
                            0:64, (grp % 2) * 512:(grp % 2) * 512 + 512]
                        s.emit("vector", lambda e, dst_sc=dst_sc: e.tensor_tensor(
                            out=dst_sc, in0=ps_sc[0:64, 0:512],
                            in1=mask8[0:64, 0:512], op=ALU.mult), inc="dve")
                        s.wait("tensor", "dve", s.val("dve"))
                xp_src = Xp_tm if ci == 0 else Xp_odd
                for h in range(NH):
                    bank = ps_y1 if h < 16 else ps_y2
                    bcol = (h % 16) * HD
                    sc_src = (sc_sb_a if h < 16 else sc_sb_b)[
                        0:64, (h % 16) * 64:(h % 16) * 64 + 64]
                    s.emit("tensor", lambda e, sc_src=sc_src, bank=bank, bcol=bcol,
                           h=h, xp_src=xp_src: e.matmul(
                        out=bank[0:64, bcol:bcol + HD], lhsT=sc_src,
                        rhs=xp_src[0:64, h * HD: h * HD + HD],
                        start=True, stop=False), inc="pe")
                    if h % 2 == 0:
                        lc = C_fm[0:64, (h // 2) * 512 + clt:(h // 2) * 512 + clt + 64]
                    else:
                        lc = C_odd[0:64, (h // 2) * 128 + po:(h // 2) * 128 + po + 64]
                    s.emit("tensor", lambda e, lc=lc, bank=bank, bcol=bcol, h=h:
                           e.matmul(
                        out=bank[0:64, bcol:bcol + HD], lhsT=lc,
                        rhs=S_bf[0:64, h * HD: h * HD + HD],
                        start=False, stop=True), inc="pe")
                vy = s.val("pe")
                epv = epos_all[0:64, cg * NH:(cg + 1) * NH]
                s.wait("vector", "pe", vy)
                dv = drep[0:64, :].unsqueeze(2).broadcast_to([64, NH, HD])
                s.emit("vector", lambda e, xp_src=xp_src, dv=dv: e.tensor_tensor(
                    out=yD[0:64, :].rearrange("p (h d) -> p h d", d=HD),
                    in0=xp_src[0:64, 0:DI].rearrange("p (h d) -> p h d", d=HD),
                    in1=dv, op=ALU.mult), inc="dve")
                for (bank, h0) in ((ps_y1, 0), (ps_y2, 16)):
                    s.emit("vector", lambda e, bank=bank, h0=h0: e.tensor_tensor(
                        out=y_sb[0:64, h0 * HD:(h0 + 16) * HD],
                        in0=bank[0:64, 0:512], in1=yD[0:64, h0 * HD:(h0 + 16) * HD],
                        op=ALU.add), inc="dve")
                ev = epv.unsqueeze(2).broadcast_to([64, NH, HD])
                s.emit("vector", lambda e, ev=ev: e.tensor_tensor(
                    out=yD[0:64, :].rearrange("p (h d) -> p h d", d=HD),
                    in0=y_sb[0:64, :].rearrange("p (h d) -> p h d", d=HD),
                    in1=ev, op=ALU.mult), inc="dve")
                sz_src = (sz_t[0:64, sub * DM:(sub + 1) * DM] if ci == 0
                          else sz_odd[0:64, 0:DM])
                s.emit("vector", lambda e, sz_src=sz_src: e.tensor_tensor(
                    out=yz_tm[0:64, :], in0=yD[0:64, :], in1=sz_src,
                    op=ALU.mult), inc="dve")
                vyz = s.val("dve")
                s.wait("tensor", "dve", vyz)
                btm_src = B_tm if ci == 0 else B_tm_odd
                for h in range(NH):
                    bank = ps_y1 if h < 16 else ps_y2
                    s.emit("tensor", lambda e, h=h, btm_src=btm_src, xp_src=xp_src,
                           bank=bank: e.matmul(
                        out=bank[0:64, (h % 16) * HD:(h % 16) * HD + HD],
                        lhsT=btm_src[0:64, h * DST: h * DST + DST],
                        rhs=xp_src[0:64, h * HD: h * HD + HD],
                        start=True, stop=True), inc="pe")
                vst = s.val("pe")
                dcrow = dc0_all[0:1, cg * NH:(cg + 1) * NH]
                for half in range(2):
                    rv = dcrow[:, half * 16:(half + 1) * 16].unsqueeze(2).broadcast_to(
                        [1, 16, HD])
                    s.emit("tensor", lambda e, rv=rv, half=half: e.matmul(
                        out=ps_sc[0:64, 0:512] if half == 0 else ps_sc[0:64, 0:512],
                        lhsT=ones_r[:, 0:64], rhs=rv, start=True, stop=True), inc="pe")
                    s.wait("vector", "pe", s.val("pe"))
                    s.emit("vector", lambda e, half=half: e.tensor_copy(
                        out=dcfac[0:64, half * 512:(half + 1) * 512],
                        in_=ps_sc[0:64, 0:512]), inc="dve")
                    s.wait("tensor", "dve", s.val("dve"))
                s.wait("vector", "pe", vst)
                s.emit("vector", lambda e: e.tensor_tensor(
                    out=S_tmp[0:64, 0:512], in0=ps_y1[0:64, 0:512],
                    in1=S_f[0:64, 0:512], op=ALU.add), inc="dve")
                s.emit("vector", lambda e: e.tensor_tensor(
                    out=S_tmp[0:64, 512:1024], in0=ps_y2[0:64, 0:512],
                    in1=S_f[0:64, 512:1024], op=ALU.add), inc="dve")
                s.emit("vector", lambda e: e.tensor_tensor(
                    out=S_f[0:64, :], in0=S_tmp[0:64, :], in1=dcfac[0:64, :],
                    op=ALU.mult), inc="dve")
                s.emit("vector", lambda e: e.tensor_copy(out=S_bf[0:64, :],
                                                         in_=S_f[0:64, :]), inc="dve")
                s.emit("vector", lambda e, dcrow=dcrow: e.tensor_tensor(
                    out=prow[0:1, :], in0=Ptot[0:1, :], in1=dcrow, op=ALU.mult),
                    inc="dve")
                s.emit("vector", lambda e: e.tensor_copy(out=Ptot[0:1, :],
                                                         in_=prow[0:1, :]), inc="dve")
                s.wait("tensor", "dve", s.val("dve"))
                for fc in range(8):
                    s.emit("tensor", lambda e, fc=fc: e.transpose(
                        out=ps_tr[:, 0:64], in_=yz_tm[0:64, fc * 128:(fc + 1) * 128],
                        identity=ident[0:64, 0:64]), inc="pe")
                    s.wait("vector", "pe", s.val("pe"))
                    s.emit("vector", lambda e, fc=fc: e.tensor_copy(
                        out=yz_fm[:, fc * 64:(fc + 1) * 64], in_=ps_tr[:, 0:64]),
                        inc="dve")
                    s.wait("tensor", "dve", s.val("dve"))
                for half in range(2):
                    for kq in range(8):
                        s.emit("tensor", lambda e, kq=kq, half=half: e.matmul(
                            out=(ps_c if half == 0 else ps_d)[0:64, 0:512],
                            lhsT=yz_fm[:, kq * 64:(kq + 1) * 64],
                            rhs=woutbf[:, kq * DM + half * 512:
                                       kq * DM + half * 512 + 512],
                            start=(kq == 0), stop=(kq == 7)), inc="pe")
                vop = s.val("pe")
                s.wait("sync", "dve", ut_guard[0])
                ct0g = stok + po
                dma_s(u_t[0:64, 0:DM], d_u[3 + ct0g:3 + ct0g + 64, :])
                s.wait("vector", "dq_s", s.val("dq_s"))
                s.wait("vector", "dq_g", s.val("dq_g"))
                s.wait("vector", "pe", vop)
                s.emit("vector", lambda e: e.tensor_tensor(
                    out=o_sb[0:64, 0:512], in0=ps_c[0:64, 0:512],
                    in1=u_t[0:64, 0:512], op=ALU.add), inc="dve")
                s.emit("vector", lambda e: e.tensor_tensor(
                    out=o_sb[0:64, 512:1024], in0=ps_d[0:64, 0:512],
                    in1=u_t[0:64, 512:1024], op=ALU.add), inc="dve")
                ut_guard[0] = s.val("dve")
                s.wait("gpsimd", "dve", s.val("dve"))
                dma_g(d_o[ct0g:ct0g + 64, :], o_sb[0:64, 0:DM])
                s.wait("tensor", "dve", s.val("dve"))
        barrier()

    mark("S")
    dma_g(d_scin[0, 0:64 * 1024].rearrange("(p c) -> p c", c=1024), S_f[0:64, :])
    dma_g(d_scin[0, 64 * 1024:64 * 1024 + NH].rearrange("(p c) -> p c", c=NH),
          Ptot[0:1, :])
    s.wait("gpsimd", "dq_g", s.val("dq_g"))
    s.emit("gpsimd", lambda e: e.collective_compute(
        "AllGather", ALU.bypass, replica_groups=[list(range(NCORE))],
        ins=[d_scin[:]], outs=[d_scout[:]]), inc="gps")
    vag = s.val("gps")
    s.wait("sync", "gps", vag)
    for j in range(NCORE):
        dma_s(gathP[0:1, j * NH:(j + 1) * NH],
              d_scout[j, 64 * 1024:64 * 1024 + NH].rearrange("(p c) -> p c", c=NH))
    s.wait("vector", "dq_s", s.val("dq_s"))
    s.emit("vector", lambda e: e.memset(Sin_f[0:64, :], 0.0), inc="dve")
    for j in range(NCORE):
        s.wait("sync", "dve", s.val("dve"))
        dma_s(gath_j[0:64, 0:1024],
              d_scout[j, 0:64 * 1024].rearrange("(p c) -> p c", c=1024))
        s.emit("vector", lambda e, j=j: e.tensor_scalar(
            out=prow[0:1, :], in0=gathP[0:1, j * NH:(j + 1) * NH],
            scalar1=chain[0:1, j:j + 1], scalar2=chain[0:1, 8 + j:9 + j],
            op0=ALU.mult, op1=ALU.add), inc="dve")
        s.wait("tensor", "dve", s.val("dve"))
        for half in range(2):
            rv = prow[0:1, half * 16:(half + 1) * 16].unsqueeze(2).broadcast_to(
                [1, 16, HD])
            s.emit("tensor", lambda e, rv=rv: e.matmul(
                out=ps_sc[0:64, 0:512], lhsT=ones_r[:, 0:64], rhs=rv,
                start=True, stop=True), inc="pe")
            s.wait("vector", "pe", s.val("pe"))
            s.emit("vector", lambda e, half=half: e.tensor_copy(
                out=dcfac[0:64, half * 512:(half + 1) * 512],
                in_=ps_sc[0:64, 0:512]), inc="dve")
            s.wait("tensor", "dve", s.val("dve"))
        s.emit("vector", lambda e: e.tensor_tensor(
            out=Drun[0:64, :], in0=Sin_f[0:64, :], in1=dcfac[0:64, :],
            op=ALU.mult), inc="dve")
        s.emit("vector", lambda e, j=j: e.tensor_scalar(
            out=prow[0:1, :], in0=gathP[0:1, j * NH:(j + 1) * NH],
            scalar1=0.0, scalar2=chain[0:1, 16 + j:17 + j],
            op0=ALU.mult, op1=ALU.add), inc="dve")
        s.wait("tensor", "dve", s.val("dve"))
        for half in range(2):
            rv = prow[0:1, half * 16:(half + 1) * 16].unsqueeze(2).broadcast_to(
                [1, 16, HD])
            s.emit("tensor", lambda e, rv=rv: e.matmul(
                out=ps_sc[0:64, 0:512], lhsT=ones_r[:, 0:64], rhs=rv,
                start=True, stop=True), inc="pe")
            s.wait("vector", "pe", s.val("pe"))
            s.emit("vector", lambda e, half=half: e.tensor_copy(
                out=dcfac[0:64, half * 512:(half + 1) * 512],
                in_=ps_sc[0:64, 0:512]), inc="dve")
            s.wait("tensor", "dve", s.val("dve"))
        s.wait("vector", "dq_s", s.val("dq_s"))
        s.emit("vector", lambda e: e.tensor_tensor(
            out=S_tmp[0:64, :], in0=gath_j[0:64, 0:1024], in1=dcfac[0:64, :],
            op=ALU.mult), inc="dve")
        s.emit("vector", lambda e: e.tensor_tensor(
            out=Sin_f[0:64, :], in0=Drun[0:64, :], in1=S_tmp[0:64, :],
            op=ALU.add), inc="dve")
        s.wait("tensor", "dve", s.val("dve"))
    s.emit("vector", lambda e: e.memset(Drun[0:64, :], 1.0), inc="dve")
    barrier()

    mark("D")
    dC_guard = [0]
    for c in range(NCHUNK):
        ct0 = c * QC
        s.emit("vector", lambda e: e.tensor_tensor(
            out=Sd_bf[0:64, :], in0=Sin_f[0:64, :], in1=Drun[0:64, :],
            op=ALU.mult), inc="dve")
        s.wait("sync", "pe", dC_guard[0])
        for h in range(NH):
            fc = h // 2
            r0 = (h % 2) * 64
            dma_s(dC[0:64, h * 64:(h + 1) * 64], d_cs[fc, r0:r0 + 64, ct0:ct0 + QC])
        dma_s(dsz[0:64, 0:1024],
              d_sz[ct0 // 128, (c % 2) * 64:(c % 2) * 64 + 64, 0:DM])
        s.wait("tensor", "dq_s", s.val("dq_s"))
        s.wait("tensor", "dve", s.val("dve"))
        for h in range(NH):
            bank = ps_y1 if h < 16 else ps_y2
            s.emit("tensor", lambda e, h=h, bank=bank: e.matmul(
                out=bank[0:64, (h % 16) * HD:(h % 16) * HD + HD],
                lhsT=dC[0:64, h * 64:(h + 1) * 64],
                rhs=Sd_bf[0:64, h * HD:h * HD + HD],
                start=True, stop=True), inc="pe")
        dC_guard[0] = s.val("pe")
        epv = epos_all[0:64, c * NH:(c + 1) * NH]
        s.wait("vector", "pe", s.val("pe"))
        for (bank, h0) in ((ps_y1, 0), (ps_y2, 16)):
            ev = epv[:, h0:h0 + 16].unsqueeze(2).broadcast_to([64, 16, HD])
            s.emit("vector", lambda e, bank=bank, h0=h0, ev=ev: e.tensor_tensor(
                out=dy_sb[0:64, h0 * HD:(h0 + 16) * HD].rearrange(
                    "p (h d) -> p h d", d=HD),
                in0=bank[0:64, 0:512].rearrange("p (h d) -> p h d", d=HD),
                in1=ev, op=ALU.mult), inc="dve")
        s.emit("vector", lambda e: e.tensor_tensor(
            out=y_sb[0:64, 0:DI], in0=dy_sb[0:64, 0:DI], in1=dsz[0:64, 0:DI],
            op=ALU.mult), inc="dve")
        s.wait("tensor", "dve", s.val("dve"))
        for fc in range(8):
            s.emit("tensor", lambda e, fc=fc: e.transpose(
                out=ps_tr[:, 0:64], in_=y_sb[0:64, fc * 128:(fc + 1) * 128],
                identity=ident[0:64, 0:64]), inc="pe")
            s.wait("vector", "pe", s.val("pe"))
            s.emit("vector", lambda e, fc=fc: e.tensor_copy(
                out=yz_fm[:, fc * 64:(fc + 1) * 64], in_=ps_tr[:, 0:64]), inc="dve")
            s.wait("tensor", "dve", s.val("dve"))
        for half in range(2):
            for kq in range(8):
                s.emit("tensor", lambda e, kq=kq, half=half: e.matmul(
                    out=(ps_c if half == 0 else ps_d)[0:64, 0:512],
                    lhsT=yz_fm[:, kq * 64:(kq + 1) * 64],
                    rhs=woutbf[:, kq * DM + half * 512: kq * DM + half * 512 + 512],
                    start=(kq == 0), stop=(kq == 7)), inc="pe")
        vdo = s.val("pe")
        s.wait("sync", "dq_g", s.val("dq_g"))
        s.wait("sync", "dve", ut_guard[0])
        dma_s(u_t[0:64, 0:DM], d_o[ct0:ct0 + QC, :], sem="dq1")
        s.wait("vector", "dq1", s.val("dq1"))
        s.wait("vector", "pe", vdo)
        s.emit("vector", lambda e: e.tensor_tensor(
            out=o_sb[0:64, 0:512], in0=u_t[0:64, 0:512], in1=ps_c[0:64, 0:512],
            op=ALU.add), inc="dve")
        s.emit("vector", lambda e: e.tensor_tensor(
            out=o_sb[0:64, 512:1024], in0=u_t[0:64, 512:1024],
            in1=ps_d[0:64, 0:512], op=ALU.add), inc="dve")
        ut_guard[0] = s.val("dve")
        s.wait("gpsimd", "dve", s.val("dve"))
        dma_g(d_o[ct0:ct0 + QC, :], o_sb[0:64, 0:DM])
        s.wait("tensor", "dve", s.val("dve"))
        dcrow = dc0_all[0:1, c * NH:(c + 1) * NH]
        for half in range(2):
            rv = dcrow[:, half * 16:(half + 1) * 16].unsqueeze(2).broadcast_to(
                [1, 16, HD])
            s.emit("tensor", lambda e, rv=rv: e.matmul(
                out=ps_sc[0:64, 0:512], lhsT=ones_r[:, 0:64], rhs=rv,
                start=True, stop=True), inc="pe")
            s.wait("vector", "pe", s.val("pe"))
            s.emit("vector", lambda e, half=half: e.tensor_copy(
                out=dcfac[0:64, half * 512:(half + 1) * 512],
                in_=ps_sc[0:64, 0:512]), inc="dve")
            s.wait("tensor", "dve", s.val("dve"))
        s.emit("vector", lambda e: e.tensor_tensor(
            out=S_tmp[0:64, :], in0=Drun[0:64, :], in1=dcfac[0:64, :],
            op=ALU.mult), inc="dve")
        s.emit("vector", lambda e: e.tensor_copy(out=Drun[0:64, :],
                                                 in_=S_tmp[0:64, :]), inc="dve")
        s.wait("tensor", "dve", s.val("dve"))
    barrier()

    mark("F")
    s.wait("sync", "gps", v_wq_gathered)
    nblocks = [(i * 512, 512) for i in range(5)] + [(2560, 256)]
    wst_guard = [0]
    for mega in range(2):
        mt0 = mega * 1024
        for sub in range(8):
            stok = mt0 + sub * 128
            s.wait("sync", "pe", wst_guard[0])
            for kq in range(8):
                dma_s(xq_st[:, kq * 128:(kq + 1) * 128],
                      d_xq[kq, :, stok:stok + 128], sem="dq2")
            s.wait("tensor", "dq2", s.val("dq2"))
            for (n0, nn) in nblocks:
                for kq in range(8):
                    s.wait("sync", "pe", wst_guard[0])
                    j = n0
                    while j < n0 + nn:
                        r = j // WSG
                        lc = j % WSG
                        take = min(WSG - lc, n0 + nn - j)
                        dma_s(wq_st[:, (j - n0):(j - n0) + take],
                              d_wcout[r, 0:DM * WSG].rearrange(
                                  "(k w) -> k w", w=WSG)[kq * 128:(kq + 1) * 128,
                                                         lc:lc + take])
                        dma_s(wq_st[:, 1024 + (j - n0):1024 + (j - n0) + take],
                              d_wcout[r, DM * WSG:2 * DM * WSG].rearrange(
                                  "(k w) -> k w", w=WSG)[kq * 128:(kq + 1) * 128,
                                                         lc:lc + take])
                        j += take
                    s.wait("tensor", "dq_s", s.val("dq_s"))
                    s.emit("tensor", lambda e, kq=kq, nn=nn: e.matmul(
                        out=ps_a[:, 0:nn],
                        lhsT=xq_st[:, kq * 128:(kq + 1) * 128],
                        rhs=wq_st[:, 0:nn], start=(kq == 0), stop=(kq == 7)), inc="pe")
                    s.emit("tensor", lambda e, kq=kq, nn=nn: e.matmul(
                        out=ps_b[:, 0:nn],
                        lhsT=xq_st[:, kq * 128:(kq + 1) * 128],
                        rhs=wq_st[:, 1024:1024 + nn], start=(kq == 0), stop=(kq == 7)),
                        inc="pe")
                    wst_guard[0] = s.val("pe")
                vgu = s.val("pe")
                s.wait("scalar", "pe", vgu)
                s.emit("scalar", lambda e, nn=nn: e.activation(
                    out=sg_sb[:, 0:nn], in_=ps_a[:, 0:nn], func=AF.Silu), inc="act")
                s.wait("vector", "act", s.val("act"))
                s.emit("vector", lambda e, nn=nn: e.tensor_tensor(
                    out=h_f[:, 0:nn], in0=ps_b[:, 0:nn], in1=sg_sb[:, 0:nn],
                    op=ALU.mult), inc="dve")
                s.wait("tensor", "dve", s.val("dve"))
                quant_pipe(h_f[:, 0:nn], hq_bf[:, 0:nn], nn, f16=True)
                s.wait("tensor", "dve", s.val("dve"))
                for fb in range(nn // 128):
                    kcg = (n0 + fb * 128) // 128
                    s.emit("tensor", lambda e, fb=fb: e.transpose(
                        out=ps_tr[:, 0:128], in_=hq_bf[:, fb * 128:(fb + 1) * 128],
                        identity=ident), inc="pe")
                    s.wait("vector", "pe", s.val("pe"))
                    s.emit("vector", lambda e, kcg=kcg, sub=sub: e.tensor_copy(
                        out=hq_fm[:, kcg * 1024 + sub * 128:
                                  kcg * 1024 + sub * 128 + 128],
                        in_=ps_tr[:, 0:128]), inc="dve")
                    s.wait("tensor", "dve", s.val("dve"))
                if DIAG_FBAR:
                    barrier()
        for sub in range(8):
            stok = mt0 + sub * 128
            for half in range(2):
                for kq in range(22):
                    s.wait("sync", "pe", wst_guard[0])
                    for pc in range(4):
                        jj = half * 512 + pc * 128
                        r = jj // WSD
                        lc = jj % WSD
                        dma_s(wq_st[:, pc * 128:(pc + 1) * 128],
                              d_wcout[r, 2 * DM * WSG:2 * DM * WSG + IFF * WSD]
                              .rearrange("(k w) -> k w", w=WSD)[
                                  kq * 128:(kq + 1) * 128, lc:lc + 128])
                    s.wait("tensor", "dq_s", s.val("dq_s"))
                    s.emit("tensor", lambda e, kq=kq, sub=sub, half=half: e.matmul(
                        out=(ps_c if half == 0 else ps_d)[:, 0:512],
                        lhsT=hq_fm[:, kq * 1024 + sub * 128:
                                   kq * 1024 + sub * 128 + 128],
                        rhs=wq_st[:, 0:512], start=(kq == 0), stop=(kq == 21)),
                        inc="pe")
                    wst_guard[0] = s.val("pe")
            vdn = s.val("pe")
            s.wait("sync", "dq_g", s.val("dq_g"))
            s.wait("sync", "dve", ut_guard[0])
            dma_s(u_t[:, 0:DM], d_o[stok:stok + 128, :], sem="dq1")
            s.wait("vector", "dq1", s.val("dq1"))
            s.wait("vector", "pe", vdn)
            s.emit("vector", lambda e: e.tensor_tensor(
                out=o_sb[:, 0:512], in0=u_t[:, 0:512], in1=ps_c[:, 0:512],
                op=ALU.add), inc="dve")
            s.emit("vector", lambda e: e.tensor_tensor(
                out=o_sb[:, 512:1024], in0=u_t[:, 512:1024], in1=ps_d[:, 0:512],
                op=ALU.add), inc="dve")
            ut_guard[0] = s.val("dve")
            s.wait("gpsimd", "dve", s.val("dve"))
            dma_g(d_o[stok:stok + 128, :], o_sb[:, 0:DM])
            s.wait("tensor", "dve", s.val("dve"))
            if DIAG_FBAR:
                barrier()

    mark("END")
    s.wait("gpsimd", "dq_g", s.val("dq_g"))
    s.wait("gpsimd", "dq_s", s.val("dq_s"))

    if stop_after is not None:
        nxt = {"0": "W", "W": "A", "A": "B", "B": "S", "S": "D", "D": "F",
               "F": "END"}.get(stop_after, stop_after)
        cut = phase_marks[nxt]
        for e in ENGS:
            s.ops[e] = s.ops[e][:cut[e]]
        def _clamp(e):
            out = []
            for it in s.ops[e]:
                if it[0] == "wait":
                    _, nm, v = it
                    lim = s.val(nm)
                    out.append(("wait", nm, min(v, lim)))
                else:
                    out.append(it)
            return out
        tot = {}
        for e in ENGS:
            for it in s.ops[e]:
                if it[0] == "op" and it[2] is not None:
                    tot[it[2]] = tot.get(it[2], 0) + it[3]
        def _clamp2(e):
            out = []
            for it in s.ops[e]:
                if it[0] == "wait":
                    _, nm, v = it
                    out.append(("wait", nm, min(v, tot.get(nm, 0))))
                else:
                    out.append(it)
            return out
        for e in ENGS:
            s.ops[e] = _clamp2(e)
        s.ops["gpsimd"].append(("wait", "dq_g", tot.get("dq_g", 0)))
        s.ops["sync"].append(("wait", "dq_s", tot.get("dq_s", 0)))
        if dbg_dump:
            for nm in ("pe", "dve", "act", "dq_s", "dq0", "dq1", "dq2", "dq3"):
                s.ops["gpsimd"].append(("wait", nm, tot.get(nm, 0)))
            if "__ldo__" in dbg_dump:
                for r in range(4):
                    s.ops["gpsimd"].append(("op", (lambda e, r=r: e.dma_start(
                        out=bufs["B_fm"][:, r * 2048:(r + 1) * 2048].bitcast(F32),
                        in_=d_o[r * 128:(r + 1) * 128, 0:DM])),
                        "dq_g", 16))
                s.ops["gpsimd"].append(("wait", "dq_g",
                                        tot.get("dq_g", 0) + 64))
            if "__ldwc7__" in dbg_dump:
                for kq in range(8):
                    s.ops["gpsimd"].append(("op", (lambda e, kq=kq: e.dma_start(
                        out=bufs["B_fm"][:, kq * 1024:kq * 1024 + WSG],
                        in_=d_wcout[7, 0:DM * WSG].rearrange(
                            "(k w) -> k w", w=WSG)[kq * 128:(kq + 1) * 128,
                                                   0:WSG])), "dq_g", 16))
                s.ops["gpsimd"].append(("wait", "dq_g",
                                        tot.get("dq_g", 0) + 128))
            if "__ldwc__" in dbg_dump:
                for r in range(8):
                    s.ops["gpsimd"].append(("op", (lambda e, r=r: e.dma_start(
                        out=bufs["B_fm"][:, r * 1024:r * 1024 + WSG],
                        in_=d_wcout[r, 0:DM * WSG].rearrange(
                            "(k w) -> k w", w=WSG)[0:128, 0:WSG])), "dq_g", 16))
                    s.ops["gpsimd"].append(("op", (lambda e, r=r: e.dma_start(
                        out=bufs["C_fm"][:, r * 1024:r * 1024 + WSG],
                        in_=d_wcout[r, DM * WSG:2 * DM * WSG].rearrange(
                            "(k w) -> k w", w=WSG)[0:128, 0:WSG])), "dq_g", 16))
                    s.ops["gpsimd"].append(("op", (lambda e, r=r: e.dma_start(
                        out=bufs["X_fm"][:, r * 512:r * 512 + WSD],
                        in_=d_wcout[r, 2 * DM * WSG:2 * DM * WSG + IFF * WSD]
                        .rearrange("(k w) -> k w", w=WSD)[0:128, 0:WSD])),
                        "dq_g", 16))
                s.ops["gpsimd"].append(("wait", "dq_g",
                                        tot.get("dq_g", 0) + 384))
            if "__ldwinbf__" in dbg_dump:
                for kq in range(8):
                    s.ops["gpsimd"].append(("op", (lambda e, kq=kq: e.dma_start(
                        out=bufs["B_fm"][:, kq * 1024:(kq + 1) * 1024],
                        in_=d_winbf[kq * 128:(kq + 1) * 128, 3968:4992])),
                        "dq_g", 16))
                s.ops["gpsimd"].append(("wait", "dq_g",
                                        tot.get("dq_g", 0) + 128))
            col = 0
            extra = 0
            for name in dbg_dump:
                if name.startswith("__"):
                    continue
                ap = bufs[name]
                p, f = ap.shape[0], ap.shape[1]
                if ap.dtype != F32:
                    ap = ap[0:p, 0:(f // 2) * 2].bitcast(F32)
                    f = f // 2
                f = min(f, 16384 - col)
                if f <= 0:
                    continue
                s.ops["gpsimd"].append(("op", (lambda e, ap=ap, col=col, p=p, f=f:
                    e.dma_start(out=d_dbg[0:p, col:col + f], in_=ap[0:p, 0:f])),
                    "dq_g", 16))
                extra += 16
                col += f
            s.ops["gpsimd"].append(("wait", "dq_g", tot.get("dq_g", 0) + extra))

    def run_stream(eng_name, proxy):
        dve = eng_name in ("vector", "scalar")
        pe = eng_name == "tensor"
        pend = [None, 0]

        def flush():
            if pend[0] is not None:
                proxy.drain().then_inc(sems[pend[0]], pend[1])
                pend[0] = None
                pend[1] = 0

        for item in s.ops[eng_name]:
            if item[0] == "wait":
                if pe:
                    flush()
                _, sem_name, val = item
                proxy.wait_ge(sems[sem_name], val)
            else:
                _, fn, inc, amt = item
                inst = fn(proxy)
                if dve and not getattr(inst, "is_dma", False):
                    d = proxy.drain()
                    if inc is not None:
                        d.then_inc(sems[inc], amt)
                elif pe and not getattr(inst, "is_dma", False):
                    if inc is not None:
                        if pend[0] is not None and pend[0] != inc:
                            flush()
                        pend[0] = inc
                        pend[1] += amt
                elif inc is not None:
                    inst.then_inc(sems[inc], amt)
        if pe:
            flush()

    @blk.sync
    def _(e):
        run_stream("sync", e)

    @blk.tensor
    def _(e):
        run_stream("tensor", e)

    @blk.vector
    def _(e):
        run_stream("vector", e)

    @blk.scalar
    def _(e):
        run_stream("scalar", e)

    @blk.gpsimd
    def _(e):
        run_stream("gpsimd", e)

    for h in reversed(stack):
        h.__exit__(None, None, None)
    return nc


def _host_inputs(core, inp):
    b, q = core // 4, core % 4
    u = np.asarray(inp["L0_attn_u"], np.float32)
    t0 = q * TOK
    us = np.zeros((2051, DM), np.float32)
    lo = max(0, t0 - 3)
    us[3 - (t0 - lo):, :] = u[b, lo:t0 + TOK, :]
    W_in = np.asarray(inp["L0_attn_W_in"], np.float32)
    W_out = np.asarray(inp["L0_attn_W_out"], np.float32)
    convw = np.asarray(inp["L0_attn_conv_w"], np.float32)
    convb = np.asarray(inp["L0_attn_conv_b"], np.float32)
    cw = convw.reshape(40, 128, 4).transpose(1, 0, 2).reshape(128, 160)
    cb = np.ascontiguousarray(convb.reshape(40, 128).T)
    zb = np.broadcast_to(np.asarray(inp["L0_attn_z_bias"], np.float32)[None, :],
                         (128, DM)).copy()
    dr = np.broadcast_to(np.asarray(inp["L0_attn_D"], np.float32)[None, :],
                         (128, NH)).copy()
    ident = np.eye(128, dtype=ml_dtypes.bfloat16)
    ident2 = np.concatenate([np.eye(64), np.eye(64)], axis=0).astype(ml_dtypes.bfloat16)
    lt = np.tril(np.ones((64, 64), np.float32)).T
    lt2 = np.concatenate([lt, lt], axis=0)
    tril_T = np.tril(np.ones((QC, QC), np.float32)).T
    mask8 = np.tile(tril_T, (2, 8)).astype(ml_dtypes.bfloat16)
    ones_r = np.ones((1, 128), np.float32)
    m = np.zeros(8, np.float32)
    uu = np.zeros(8, np.float32)
    for j in range(4 * b, core):
        m[j] = 1.0
        uu[j] = 1.0
    ch = np.zeros((1, 24), np.float32)
    ch[0, 0:8] = m
    ch[0, 8:16] = 1.0 - m
    ch[0, 16:24] = uu
    wg = np.asarray(inp["L0_ffn_w_gate"], np.float32)[core * WSG:(core + 1) * WSG]
    wu = np.asarray(inp["L0_ffn_w_up"], np.float32)[core * WSG:(core + 1) * WSG]
    wd = np.asarray(inp["L0_ffn_w_down"], np.float32)[core * WSD:(core + 1) * WSD]
    return {
        "u_slice": us,
        "W_inT": np.ascontiguousarray(W_in.T),
        "W_outT": np.ascontiguousarray(W_out.T),
        "conv_w": cw, "conv_b": cb, "zbias_rep": zb, "D_rep": dr,
        "in_ident": ident, "in_ident2": ident2, "in_lt64": lt2, "in_mask8": mask8, "in_ones_row": ones_r,
        "in_chain": ch,
        "wg_slice": np.ascontiguousarray(wg),
        "wu_slice": np.ascontiguousarray(wu),
        "wd_slice": np.ascontiguousarray(wd),
    }


def kernel(**inputs):
    nc = build_program()
    in_maps = [_host_inputs(k, inputs) for k in range(NCORE)]
    res = run_bass_kernel_spmd(nc, in_maps, core_ids=list(range(NCORE)))
    out = np.zeros((B, L, DM), np.float32)
    for k in range(NCORE):
        b, q = k // 4, k % 4
        out[b, q * TOK:(q + 1) * TOK, :] = res.results[k]["o"]
    return out
